# version 01
# baseline (speedup 1.0000x reference)
import numpy as np
import ml_dtypes
import concourse.bass as bass
import concourse.mybir as mybir
from concourse.bass_utils import run_bass_kernel_spmd

F32 = mybir.dt.float32
BF16 = mybir.dt.bfloat16
AF = mybir.ActivationFunctionType
ALU = mybir.AluOpType
AX = mybir.AxisListType
BF = ml_dtypes.bfloat16

import os
T = 4096; D = 1024; H = 16; DKD = 64; DV = 128; VAL = 2048; HD = 4096
TSEG = 512; NSEG = int(os.environ.get("K_NSEG", T // TSEG)); C = 128
NCH = TSEG // C
SIMSAFE = os.environ.get("K_SIMSAFE", "0") == "1"
KCUT = float(os.environ.get("K_CUT", "999999"))


class _StopBuild(Exception):
    pass


KCUTSEG = int(os.environ.get("K_CUTSEG", "0"))
_CURSEG = [0]


def _cut(n):
    if _CURSEG[0] >= KCUTSEG and KCUT <= n:
        raise _StopBuild()
TBAR = int(os.environ.get("K_TBAR", "0"))
DBGSEG = int(os.environ.get("K_DBGSEG", "0"))
MAGIC = 12582912.0
NEG = -1e30
DBG = False


class Sched:
    CENG = ("T", "V", "A")
    STREAMS = {"SC": "S", "SX": "S", "SO": "S", "SS0": "S", "SS1": "S",
               "SD": "S", "SL0": "S", "SL1": "S", "SCR": "S"}

    def __init__(self, nc):
        self.nc = nc
        names = list(self.CENG) + list(self.STREAMS)
        self.sem = {n: nc.alloc_semaphore(name=f"sch{n}") for n in names}
        self.cnt = {n: 0 for n in names}
        self.cnt["S"] = 0
        self.cnt["P"] = 0
        self.last_rg = None
        self.ops = {e: [] for e in ("T", "V", "A", "S", "P")}

    def _op(self, e, emit, waits, inc, key=None):
        key = key or e
        w = {}
        for tok in (waits or []):
            if tok is None:
                continue
            toks = tok if isinstance(tok, list) else [tok]
            for t2 in toks:
                if t2 is None:
                    continue
                ee, v = t2
                if ee == e and ee in self.CENG and not SIMSAFE:
                    continue
                w[ee] = max(w.get(ee, 0), v)
        self.ops[e].append((w, emit, inc, key))
        self.cnt[key] += inc
        return (key, self.cnt[key])

    def t(self, emit, waits=None, rg=(0, 128)):
        if TBAR:
            self.ops["T"].append(({"T": self.cnt["T"]}, None, 0, "T"))
        elif self.last_rg is not None and rg is not None:
            a0, a1 = self.last_rg
            b0, b1 = rg
            if a0 + a1 <= b0 or b0 + b1 <= a0:
                self.ops["T"].append(({"T": self.cnt["T"]}, None, 0, "T"))
        self.last_rg = rg
        return self._op("T", emit, waits, 1)

    def v(self, emit, waits=None):
        return self._op("V", emit, waits, 1)

    def a(self, emit, waits=None):
        return self._op("A", emit, waits, 1)

    def dma(self, stream, emit, waits=None):
        e = self.STREAMS[stream]
        return self._op(e, emit, waits, 16, stream)

    def wait_only(self, e, waits):
        return self._op(e, None, waits, 0)

    def verify(self):
        idx = {e: 0 for e in self.ops}
        val = {n: 0 for n in self.cnt}
        progress = True
        while progress:
            progress = False
            for e, lst in self.ops.items():
                while idx[e] < len(lst):
                    w, emit, inc, key = lst[idx[e]]
                    if all(val[n] >= v for n, v in w.items()):
                        val[key] += inc
                        idx[e] += 1
                        progress = True
                    else:
                        break
        stuck = {e: idx[e] for e in idx if idx[e] < len(self.ops[e])}
        if stuck:
            print("DEADLOCK; engine positions:",
                  {e: f"{idx[e]}/{len(self.ops[e])}" for e in self.ops})
            for e, i in stuck.items():
                w, emit, inc, key = self.ops[e][i]
                unsat = {n: (v, val[n]) for n, v in w.items() if val[n] < v}
                print(f"  {e}[{i}] key={key} unsatisfied={unsat}",
                      "emit-defaults:",
                      emit.__defaults__ and [str(d)[:80] for d in emit.__defaults__][:3]
                      if emit else None)
        else:
            print("sched verify: no deadlock")
        return not stuck

    def verify_banks(self):
        from concourse.bass_types import AP as _AP
        names = sorted(self.cnt)
        ni = {n: i for i, n in enumerate(names)}
        allops = []
        for e, lst in self.ops.items():
            cnt = {}
            for i, (w, em, inc, key) in enumerate(lst):
                cnt[key] = cnt.get(key, 0) + inc
                banks = set()
                if em is not None and em.__defaults__:
                    for d in em.__defaults__:
                        if isinstance(d, _AP) and getattr(d.tensor, "name", "") in ("ps0", "ps1"):
                            base = 0 if d.tensor.name == "ps0" else 4
                            off = int(d.offset) % 2048
                            span = 0
                            for (st, ct) in [tuple(x) for x in d.ap]:
                                if int(st) == 2048:
                                    continue
                                span += (int(ct) - 1) * int(st)
                            b0 = base + off // 512
                            b1 = base + (off + span) // 512
                            banks.update(range(b0, b1 + 1))
                allops.append([e, i, w, key, cnt[key], banks])
        import numpy as np
        NV = len(names)
        idx = {e: 0 for e in self.ops}
        prevC = {e: np.zeros(NV) for e in self.ops}
        val = {n: 0 for n in names}
        order = []
        opmap = {e: [o for o in allops if o[0] == e] for e in self.ops}
        S = {}
        progress = True
        while progress:
            progress = False
            for e in self.ops:
                while idx[e] < len(opmap[e]):
                    o = opmap[e][idx[e]]
                    w = o[2]
                    if not all(val[n] >= v for n, v in w.items()):
                        break
                    sc = prevC[e].copy()
                    for n, v in w.items():
                        sc = np.maximum(sc, 0)
                        sc[ni[n]] = max(sc[ni[n]], v)
                    C = sc.copy()
                    C[ni[o[3]]] = max(C[ni[o[3]]], o[4])
                    S[(e, o[1])] = sc
                    prevC[e] = C
                    val[o[3]] += 0
                    val[o[3]] = max(val[o[3]], o[4])
                    idx[e] += 1
                    progress = True
        comp = {}
        idx = {e: 0 for e in self.ops}
        prevC = {e: np.zeros(NV) for e in self.ops}
        val = {n: 0 for n in names}
        progress = True
        while progress:
            progress = False
            for e in self.ops:
                while idx[e] < len(opmap[e]):
                    o = opmap[e][idx[e]]
                    w = o[2]
                    if not all(val[n] >= v for n, v in w.items()):
                        break
                    sc = prevC[e].copy()
                    for n, v in w.items():
                        cc = comp.get((n, v))
                        if cc is not None:
                            sc = np.maximum(sc, cc)
                        sc[ni[n]] = max(sc[ni[n]], v)
                    C = sc.copy()
                    C[ni[o[3]]] = o[4]
                    S[(e, o[1])] = sc
                    comp[(o[3], o[4])] = C
                    prevC[e] = C
                    val[o[3]] = max(val[o[3]], o[4])
                    idx[e] += 1
                    progress = True
        tops = [o for o in allops if o[0] == "T" and o[5]]
        vops = [o for o in allops if o[0] == "V" and o[5]]
        aops = [o for o in allops if o[0] == "A" and o[5]]
        bad = 0

        def pair(t, r):
            nonlocal bad
            if not (t[5] & r[5]):
                return
            St = S[(t[0], t[1])]
            Sr = S[(r[0], r[1])]
            t_before_r = Sr[ni[t[3]]] >= t[4]
            r_before_t = St[ni[r[3]]] >= r[4]
            if not (t_before_r or r_before_t):
                bad += 1
                if bad <= 12:
                    print(f"BANK RACE: {t[0]}[{t[1]}] banks {sorted(t[5])} vs "
                          f"{r[0]}[{r[1]}] banks {sorted(r[5])}")

        for t in tops:
            for r in vops + aops:
                pair(t, r)
        for t in vops:
            for r in aops:
                pair(t, r)
        print(f"verify_banks: {bad} unordered same-bank cross-engine pairs "
              f"({len(tops)} T, {len(vops)} V, {len(aops)} A psum-ops)")
        return bad == 0

    def emit(self, block):
        sem = self.sem

        def body(ename):
            oplist = self.ops[ename]

            def run(eng):
                done = {n: 0 for n in self.cnt}
                for (w, emit, inc, key) in oplist:
                    for ee, v in sorted(w.items()):
                        if v > done[ee]:
                            eng.wait_ge(sem[ee], v)
                            done[ee] = v
                    if emit is not None:
                        ins = emit(eng)
                        ins.then_inc(sem[key], inc)
            return run

        block.tensor(body("T"))
        block.vector(body("V"))
        block.scalar(body("A"))
        block.sync(body("S"))
        block.gpsimd(body("P"))


class Buf:

    def __init__(self):
        self.w = []
        self.r = []

    def begin_write(self):
        deps = self.w + self.r
        self.w = []
        self.r = []
        return deps

    def wtok(self, tok):
        self.w.append(tok)

    def rd(self):
        return list(self.w)

    def rtok(self, tok):
        self.r.append(tok)


class Ring:

    def __init__(self, slots):
        self.slots = slots
        self.free = [[] for _ in slots]
        self.i = 0

    def acquire(self):
        i = self.i
        self.i = (self.i + 1) % len(self.slots)
        return i, self.slots[i], list(self.free[i])

    def release(self, i, toks):
        self.free[i] = [t for t in toks if t is not None]



def _consts():
    eye128 = np.eye(128, dtype=np.float32)
    c = {}
    c["I128b"] = eye128.astype(BF)
    c["I128f"] = eye128.copy()
    c["I16f"] = np.eye(16, dtype=np.float32)
    dual = np.zeros((128, 64), dtype=np.float32)
    dual[0:64] = np.eye(64)
    dual[64:128] = np.eye(64)
    c["I64dualb"] = dual.astype(BF)
    c["I64dualf"] = dual.copy()
    sel = np.zeros((16, 16, 128), dtype=np.float32)
    for h in range(16):
        sel[h, h, :] = 1.0
    c["sel16f"] = sel.copy()
    oc = np.zeros((128, 16, 16), dtype=np.float32)
    for h in range(16):
        oc[:, h, h] = 1.0
    c["oc16b"] = oc.astype(BF)
    bo = np.zeros((128, 8, 16), dtype=np.float32)
    for cc in range(8):
        bo[0:64, cc, 2 * cc] = 1.0
        bo[64:128, cc, 2 * cc + 1] = 1.0
    c["bo16b"] = bo.astype(BF)
    bc = np.zeros((16, 8, 128), dtype=np.float32)
    for cc in range(8):
        bc[2 * cc, cc, 0:64] = 1.0
        bc[2 * cc + 1, cc, 64:128] = 1.0
    c["bc16b"] = bc.astype(BF)
    bd = np.zeros((16, 1024), dtype=np.float32)
    for k in range(16):
        bd[k, k * 64:(k + 1) * 64] = 1.0
    c["bd64b"] = bd.astype(BF)
    r = np.arange(128)
    up = (r[:, None] <= r[None, :])
    c["mup"] = np.where(up, 0.0, NEG).astype(BF)
    c["mdn"] = np.where(r[:, None] > r[None, :], 0.0, NEG).astype(BF)
    s01 = (r[:, None] < r[None, :]).astype(np.float32)
    c["s01u"] = np.broadcast_to(s01[:, None, :], (128, 4, 128)).astype(BF).copy()
    c["ones1x128f"] = np.ones((1, 128), dtype=np.float32)
    c["onescol128f"] = np.ones((128, 1), dtype=np.float32)
    c["onescol128b"] = np.ones((128, 1), dtype=BF)
    return c



def build_nc():
    nc = bass.Bass("TRN2", target_bir_lowering=False,
                   dynamic_dma_scratch_size=4096)
    sch = Sched(nc)

    xT = nc.dram_tensor("xT", [D, T], F32, kind="ExternalInput").ap()
    qwT = nc.dram_tensor("qwT", [D, D], F32, kind="ExternalInput").ap()
    kwT = nc.dram_tensor("kwT", [D, D], F32, kind="ExternalInput").ap()
    vwT = nc.dram_tensor("vwT", [D, VAL], F32, kind="ExternalInput").ap()
    gwT = nc.dram_tensor("gwT", [D, VAL], F32, kind="ExternalInput").ap()
    abT = nc.dram_tensor("abT", [D, 64], F32, kind="ExternalInput").ap()
    opT = nc.dram_tensor("opT", [VAL, D], F32, kind="ExternalInput").ap()
    ouT = nc.dram_tensor("ouT", [D, D], F32, kind="ExternalInput").ap()
    w1T = nc.dram_tensor("w1T", [D, HD], F32, kind="ExternalInput").ap()
    w2T = nc.dram_tensor("w2T", [128, 8, 32, 128], F32, kind="ExternalInput").ap()
    cwI = nc.dram_tensor("cw", [128, 32, 4], F32, kind="ExternalInput").ap()
    b1I = nc.dram_tensor("b1c", [128, 32], F32, kind="ExternalInput").ap()
    b2I = nc.dram_tensor("b2c", [128, 8], F32, kind="ExternalInput").ap()
    nwrI = nc.dram_tensor("nwrow", [1, D], F32, kind="ExternalInput").ap()
    onwI = nc.dram_tensor("onw", [128, 1], F32, kind="ExternalInput").ap()
    dtbI = nc.dram_tensor("dtb", [16, 1], F32, kind="ExternalInput").ap()
    alogI = nc.dram_tensor("alog", [16, 1], F32, kind="ExternalInput").ap()
    outT = nc.dram_tensor("outT", [D, T], F32, kind="ExternalOutput").ap()

    scq = nc.dram_tensor("scq", [D, D], BF16, kind="Internal").ap()
    sck = nc.dram_tensor("sck", [D, D], BF16, kind="Internal").ap()
    scv = nc.dram_tensor("scv", [D, VAL], BF16, kind="Internal").ap()
    scg = nc.dram_tensor("scg", [D, VAL], BF16, kind="Internal").ap()
    scab = nc.dram_tensor("scab", [D, 64], BF16, kind="Internal").ap()
    scop = nc.dram_tensor("scop", [VAL, D], BF16, kind="Internal").ap()
    scou = nc.dram_tensor("scou", [D, D], BF16, kind="Internal").ap()
    scw1 = nc.dram_tensor("scw1", [D, HD], BF16, kind="Internal").ap()
    scw2 = nc.dram_tensor("scw2", [128, 8, 32, 128], BF16, kind="Internal").ap()

    cn = _consts()
    cap = {k: nc.inline_tensor(v, name="c_" + k).ap() for k, v in cn.items()}

    dbg_outs = {}
    if DBG:
        for nm, shp, dt in [
            ("d_xn", [128, 8, TSEG], BF16), ("d_q", [128, 8, TSEG], BF16),
            ("d_k", [128, 8, TSEG], BF16), ("d_v", [128, 16, TSEG], BF16),
            ("d_gate", [128, 16, TSEG], BF16), ("d_zg", [64, TSEG], F32),
            ("d_og", [128, 16, TSEG], BF16),
            ("d_sf", [128, 8, 128], F32), ("d_x1", [128, 8, TSEG], F32),
            ("d_xq", [128, 8, TSEG], BF16), ("d_hq", [128, 32, TSEG], BF16),
            ("d_m1", [128, 8, TSEG], BF16), ("d_och", [128, 16, 128], BF16),
            ("d_wsc", [1, 4], F32), ("d_Et", [128, 16, 256], BF16),
            ("d_Es", [128, 16, 128], BF16), ("d_G1", [128, 16, 128], BF16),
            ("d_H1", [128, 16, 128], BF16), ("d_At", [128, 16, 128], BF16),
            ("d_Tm0", [128, 16, 128], BF16), ("d_Tm", [128, 16, 128], BF16),
            ("d_kTW", [128, 16, 64], BF16), ("d_qL", [128, 8, 128], BF16),
            ("d_nkL", [128, 8, 128], BF16), ("d_rows", [16, 4, TSEG], F32),
            ("d_Osb", [128, 16, 128], BF16), ("d_Sb0", [128, 8, 128], BF16),
            ("d_xqa", [128, 512], F32), ("d_xqs", [128, 4], F32),
            ("d_xqp", [128, 512], F32), ("d_wme", [1, 16, 1], F32),
            ("d_wac1", [128, 1], F32), ("d_wac2", [128, 1], F32),
            ("d_wcm", [8, 128], F32), ("d_wac1b", [128, 1], F32),
            ("d_xqs2", [128, 4], F32), ("d_hh", [128, 32, TSEG], BF16),
            ("d_hst", [128, 4], F32), ("d_hsr", [1, 2, TSEG], F32),
            ("d_w1t", [128, 6, TSEG], F32), ("d_hsc", [128, TSEG], F32),
            ("d_hal", [128, 32, 3], BF16), ("d_hal2", [128, 32, 3], BF16),
            ("d_xqr", [1, 2, 512], F32), ("d_x3", [128, 8, TSEG], F32),
        ]:
            dbg_outs[nm] = nc.dram_tensor(nm, shp, dt, kind="ExternalOutput").ap()

    sb = nc.alloc_sbuf_tensor
    arena = sb("arena", [128, 57344], mybir.dt.uint8)
    abase = nc.lookup_mloc(arena).addr
    sat = nc.alloc_sbuf_tensor_at

    x0 = sb("x0", [128, 8, TSEG], F32)
    xn = sb("xn", [128, 8, TSEG], BF16)
    qb = sb("qb", [128, 8, TSEG], BF16)
    kb = sb("kb", [128, 8, TSEG], BF16)
    vb = sb("vb", [128, 16, TSEG], BF16)
    gate = sb("gate", [128, 16, TSEG], BF16)
    qpre = sb("qpre", [128, 2, TSEG + 3], BF16)
    halo = sb("halo", [128, 32, 3], BF16)
    diag = sb("diag", [128, 1, 4, 128], BF16)
    tmp0 = sb("tmp0", [128, TSEG], F32)
    tmp1 = sb("tmp1", [128, TSEG], F32)
    sqpp = sb("sqpp", [128, 2, TSEG], BF16)
    wb0 = sb("wb0", [128, 4096], BF16)
    wb1 = sb("wb1", [128, 4096], BF16)
    accq = sb("accq", [128, TSEG], F32)
    acch = sb("acch", [128, TSEG], BF16)
    hscb = sb("hscb", [128, TSEG], F32)

    hh = sat("hh", [128, 32, TSEG], BF16, offset=abase + 0)
    scr = sat("scr", [128, 8, TSEG], F32, offset=abase + 32768)
    m1 = sat("m1", [128, 8, TSEG], BF16, offset=abase + 49152)
    Et = sat("Et", [128, 16, 256], BF16, offset=abase + 0)
    Es = sat("Es", [128, 16, 128], BF16, offset=abase + 8192)
    Gc = sat("Gc", [128, 2, 16, 128], BF16, offset=abase + 12288)
    Hc = sat("Hc", [128, 2, 16, 128], BF16, offset=abase + 20480)
    At = sat("At", [128, 16, 128], BF16, offset=abase + 28672)
    Tm = sat("Tm", [128, 16, 128], BF16, offset=abase + 32768)
    Osb = sat("Osb", [128, 16, 128], BF16, offset=abase + 36864)
    och = sat("och", [128, 16, 128], BF16, offset=abase + 40960)
    osg = sat("osg", [128, 16, 128], BF16, offset=abase + 45056)
    kTW = sat("kTW", [128, 16, 64], BF16, offset=abase + 49152)
    wbc = sat("wbc", [128, 16, 64], BF16, offset=abase + 51200)
    qL = sat("qL", [128, 8, 128], BF16, offset=abase + 53248)
    nkL = sat("nkL", [128, 8, 128], BF16, offset=abase + 55296)

    IL = sb("IL", [128, 1, 64], F32)
    S_f = sb("S_f", [128, 8, 128], F32)
    S_b = sb("S_b", [128, 8, 128], BF16)

    zg = sb("zg", [64, TSEG], F32)
    g16 = sb("g16", [16, TSEG], F32)
    lnb16 = sb("lnb16", [16, TSEG], F32)
    cum16 = sb("cum16", [16, TSEG], F32)
    mln16 = sb("mln16", [16, TSEG], F32)
    wtr16 = sb("wtr16", [16, 128], F32)
    ones16 = sb("ones16", [16, TSEG], BF16)
    rt1 = sb("rt1", [16, TSEG], F32)
    rt2 = sb("rt2", [16, TSEG], F32)
    kmean_b = sb("kmean_b", [16, TSEG], BF16)
    qsc_b = sb("qsc_b", [16, TSEG], BF16)
    lamb = sb("lamb", [16, 128], BF16)
    wtrow_b = sb("wtrow_b", [16, 128], BF16)
    ctT = sb("ctT", [128, 32], F32)
    LCb = sb("LCb", [128, 16], F32)
    lcrow = sb("lcrow", [1, 16], F32)
    negA = sb("negA", [16, 1], F32)
    eps5 = sb("eps5", [128, 1], F32)
    eps6 = sb("eps6", [128, 1], F32)
    one1 = sb("one1", [128, 1], F32)
    vjunk = sb("vjunk", [128, 4], F32)
    dtb_s = sb("dtb_s", [16, 1], F32)
    alog_s = sb("alog_s", [16, 1], F32)
    onw_s = sb("onw_s", [128, 1], F32)
    nwr_s = sb("nwr_s", [1, D], F32)
    b1_s = sb("b1_s", [128, 32], F32)
    b2_s = sb("b2_s", [128, 8], F32)
    cw_s = sb("cw_s", [128, 32, 4], F32)
    m_row = sb("m_row", [1, TSEG], F32)
    var_row = sb("var_row", [1, TSEG], F32)
    r_row = sb("r_row", [1, TSEG], F32)
    r2_row = sb("r2_row", [1, TSEG], F32)
    mr_row = sb("mr_row", [1, TSEG], F32)
    xs_row = sb("xs_row", [1, TSEG], F32)
    iv_row = sb("iv_row", [1, TSEG], F32)
    hs_row = sb("hs_row", [1, TSEG], F32)
    xst = sb("xst", [128, 4], F32)
    wacc1 = sb("wacc1", [128, 1], F32)
    wacc2 = sb("wacc2", [128, 1], F32)
    wpart = sb("wpart", [128, 1], F32)
    wsc = sb("wsc", [1, 4], F32)
    winvc1 = sb("winvc1", [128, 1], F32)
    winvc2 = sb("winvc2", [128, 1], F32)
    csb = {k: sb("s_" + k, list(cn[k].shape),
                 BF16 if cn[k].dtype == BF else F32) for k in cn}

    ps0 = nc.alloc_psum_tensor("ps0", [128, 2048], F32)
    ps1 = nc.alloc_psum_tensor("ps1", [128, 2048], F32)
    ring = Ring([ps0[:, 0:1024], ps0[:, 1024:2048],
                 ps1[:, 0:1024], ps1[:, 1024:2048]])

    B = {}
    for nm in ["x0", "xn", "qb", "kb", "vb", "gate", "scr", "hh", "m1",
               "qpre0", "qpre1", "halo", "diag0", "diag1", "tmp0", "tmp1",
               "sq0", "sq1", "wb0", "wb1", "accq", "acch", "hscb",
               "Et", "Es", "Gc0", "Gc1", "Hc0", "Hc1", "At", "Tm", "kTW",
               "wbc", "qL", "nkL", "Osb", "och", "osg", "IL", "S_f", "S_b",
               "zg", "g16", "lnb16", "cum16", "mln16", "wtr16", "rt1", "rt2",
               "kmean", "qsc", "ksc", "rstdo", "lamb", "wtrb", "ctT", "LCb",
               "lcrow", "rows", "xst", "wacc", "wpart", "wsc", "xsrow", "ivrow",
               "hsrow"]:
        B[nm] = Buf()

    def mm(ps_ap, lh, rh, start, stop, tp=None):
        def e(eng, ps_ap=ps_ap, lh=lh, rh=rh, start=start, stop=stop, tp=tp):
            return eng.matmul(ps_ap, lh, rh, start=start, stop=stop,
                              tile_position=tp)
        return e

    def cpy(dst, src):
        def e(eng, dst=dst, src=src):
            return eng.tensor_copy(dst, src)
        return e

    def acpy(dst, src):
        def e(eng, dst=dst, src=src):
            return eng.activation(dst, src, AF.Identity)
        return e

    def act(dst, src, fn, bias=0.0, scale=1.0):
        def e(eng, dst=dst, src=src, fn=fn, bias=bias, scale=scale):
            return eng.activation(dst, src, fn, bias=bias, scale=scale)
        return e

    def tt(dst, a, b, op):
        def e(eng, dst=dst, a=a, b=b, op=op):
            return eng.tensor_tensor(dst, a, b, op)
        return e

    def ts(dst, a, s1, s2, op0, op1=ALU.bypass):
        def e(eng, dst=dst, a=a, s1=s1, s2=s2, op0=op0, op1=op1):
            return eng.tensor_scalar(dst, a, s1, s2, op0, op1)
        return e

    def stt(dst, a, s, b, op0, op1):
        def e(eng, dst=dst, a=a, s=s, b=b, op0=op0, op1=op1):
            return eng.scalar_tensor_tensor(dst, a, s, b, op0, op1)
        return e

    def dma(dst, src):
        def e(eng, dst=dst, src=src):
            return eng.dma_start(out=dst, in_=src)
        return e

    def memset(ap, val):
        def e(eng, ap=ap, val=val):
            return eng.memset(ap, val)
        return e

    def vgap():
        deps = [("V", sch.cnt["V"])] if SIMSAFE else []
        return sch.v(memset(vjunk[:], 0.0), deps)

    def recip(dst, src):
        def e(eng, dst=dst, src=src):
            return eng.reciprocal(dst, src)
        return e

    def quant_colmax(ri, rap, mts):
        t1d = B["tmp1"].begin_write()
        cp0 = sch.v(cpy(tmp1[:, 0:512], rap[:, 0:512]), mts + t1d)
        B["tmp1"].wtok(cp0)
        ring.release(ri, [cp0])
        t0d = B["tmp0"].begin_write()
        bufs = [tmp0, tmp1]
        tok = cp0
        deps_extra = t0d
        cur, n = tmp1, 128
        k = 0
        while n > 1:
            dst = bufs[k % 2]
            h = n // 2
            vsrc = cur[:, 0:4 * n].rearrange("p (a o) -> p a o", a=4)
            vdst = dst[:, 0:4 * h].rearrange("p (a o) -> p a o", a=4)
            tok = sch.v(tt(vdst, vsrc[:, :, 0:h], vsrc[:, :, h:n], ALU.max),
                        [tok] + deps_extra)
            deps_extra = []
            (B["tmp0"] if dst is tmp0 else B["tmp1"]).wtok(tok)
            if 4 * h < 128:
                vgap()
            if 4 * h <= 16:
                vgap()
            cur, n = dst, h
            k += 1
        xstdeps = B["xst"].begin_write()
        rd2 = sch.v(ts(xst[:, :], cur[:, 0:4], 1e-5, None, ALU.max),
                    [tok] + xstdeps)
        B["xst"].wtok(rd2)
        (B["tmp0"] if cur is tmp0 else B["tmp1"]).rtok(rd2)
        return rd2

    def red(dst, src, op, absval=False):
        def e(eng, dst=dst, src=src, op=op, absval=absval):
            return eng.tensor_reduce(dst, src, AX.X, op,
                                     apply_absolute_value=absval)
        return e

    def scan_op(dst, d0, d1, init):
        def e(eng, dst=dst, d0=d0, d1=d1, init=init):
            return eng.tensor_tensor_scan(dst, d0, d1, init, ALU.mult, ALU.add)
        return e

    _dbg_last = [None]

    def dbg_dump_slot(name, slot, src, deps):
        if not DBG or name not in dbg_outs:
            return
        tok = sch.dma("SD", dma(dbg_outs[name][:, slot, :], src),
                      list(deps) + [_dbg_last[0]])
        _dbg_last[0] = tok
        return tok

    _ARENA_OVERLAY = {"d_Et": "hh", "d_Es": "hh", "d_G1": "hh",
                      "d_H1": "hh", "d_At": "hh", "d_Tm0": "scr",
                      "d_Tm": "scr", "d_Osb": "scr", "d_och": "scr",
                      "d_kTW": "m1", "d_qL": "m1", "d_nkL": "m1"}

    def dbg_dump(name, src, deps, buf=None):
        if not DBG or name not in dbg_outs:
            return
        tok = sch.dma("SD", dma(dbg_outs[name], src),
                      list(deps) + [_dbg_last[0]])
        _dbg_last[0] = tok
        if buf is not None:
            buf.rtok(tok)
        ov = _ARENA_OVERLAY.get(name)
        if ov is not None:
            B[ov].rtok(tok)
        return tok

    ld = []
    for nm in cn:
        ld.append(sch.dma("SC", dma(csb[nm][:], cap[nm]), []))
    for dst, src in [(cw_s, cwI), (b1_s, b1I), (b2_s, b2I), (nwr_s, nwrI),
                     (onw_s, onwI), (dtb_s, dtbI), (alog_s, alogI)]:
        ld.append(sch.dma("SC", dma(dst[:], src), []))
    ldtok = ld[-1]
    consts_tok = ld

    t0 = sch.v(memset(S_f[:], 0.0), [])
    t1 = sch.v(memset(S_b[:], 0.0), [])
    t2 = sch.v(memset(halo[:], 0.0), [])
    t3 = sch.v(memset(ones16[:], 1.0), [])
    t4 = sch.v(memset(wacc1[:], 0.0), [])
    t5 = sch.v(memset(wacc2[:], 0.0), [])
    te5 = sch.v(memset(eps5[:], 1e-5), [])
    te6 = sch.v(memset(eps6[:], 1e-6), [])
    to1 = sch.v(memset(one1[:], 1.0), [])
    B["S_f"].wtok(t0); B["S_b"].wtok(t1); B["halo"].wtok(t2)
    B["wacc"].wtok(t4); B["wacc"].wtok(t5)

    negA_t = sch.a(act(negA[:], alog_s[:], AF.Exp), [ldtok] + consts_tok)
    negA_t = sch.v(ts(negA[:], negA[:], -1.0, None, ALU.mult), [negA_t])

    scrf = scr.rearrange("p a t -> p (a t)")
    wbufs = [wb0, wb1]
    wbufB = [B["wb0"], B["wb1"]]
    wslot = [0]

    def bounce(src_ap, dst_ap, shape_view):
        i = wslot[0] % 2
        wslot[0] += 1
        wbuf = wbufs[i]
        bb = wbufB[i]
        a, o = shape_view[1], shape_view[2]
        nel = a * o
        sdeps = B["scr"].begin_write()
        lt = sch.dma("SCR", dma(scrf[:, 0:nel].rearrange(
            "p (a o) -> p a o", a=a), src_ap), sdeps)
        B["scr"].wtok(lt)
        wdeps = bb.begin_write()
        view = wbuf.rearrange("p (a o) -> p a o", a=a)[:, :, 0:o]
        ct = sch.a(acpy(view, scrf[:, 0:nel].rearrange(
            "p (a o) -> p a o", a=a)), [lt] + wdeps)
        B["scr"].rtok(ct)
        bb.wtok(ct)
        st = sch.dma("SS%d" % i, dma(dst_ap, view), [ct])
        bb.rtok(st)
        return st

    precast_toks = []
    region_toks = {}
    for (rg_name, srcW, dstW, rows, cols, csz) in [
            ("q", qwT, scq, D, D, 512), ("k", kwT, sck, D, D, 512),
            ("ou", ouT, scou, D, D, 512), ("v", vwT, scv, D, VAL, 512),
            ("g", gwT, scg, D, VAL, 512), ("ab", abT, scab, D, 64, 64),
            ("op", opT, scop, VAL, D, 256)]:
        a = rows // 128
        sr = srcW.rearrange("(a p) o -> p a o", p=128)
        dr = dstW.rearrange("(a p) o -> p a o", p=128)
        for c0 in range(0, cols, csz):
            tok = bounce(sr[:, :, c0:c0 + csz], dr[:, :, c0:c0 + csz],
                         [128, a, csz])
            precast_toks.append(tok)
            region_toks.setdefault(rg_name, []).append(tok)

    w1r = w1T.rearrange("(a p) o -> p a o", p=128)

    def meanpass(tiles, pw_col, first_last):
        toks = []
        nmm = 0
        total_mm = sum((nel + 511) // 512 for (_s, nel) in tiles)
        for (src_ap, nel) in tiles:
            deps = B["scr"].begin_write()
            lt = sch.dma("SCR", dma(scrf[:, 0:nel].rearrange(
                "p (a o) -> p a o", a=src_ap.shape[1]), src_ap), deps)
            B["scr"].wtok(lt)
            nch_ = (nel + 511) // 512
            for kc in range(nch_):
                xdeps = B["xn"].begin_write()
                at_ = sch.a(act(xn[:, kc, :],
                                scrf[:, kc * 512:(kc + 1) * 512], AF.Abs),
                            [lt] + xdeps)
                B["xn"].wtok(at_)
                B["scr"].rtok(at_)
                mt_ = sch.t(mm(pw_col, csb["onescol128b"][:], xn[:, kc, :],
                               first_last[0] and nmm == 0,
                               first_last[1] and nmm == total_mm - 1),
                            [at_] + consts_tok +
                            (wsc_rfree if (first_last[0] and nmm == 0)
                             else []))
                B["xn"].rtok(mt_)
                toks.append(mt_)
                nmm += 1
        return toks[-1]

    w1tiles = [(w1r[:, :, c0:c0 + 512], 8 * 512) for c0 in range(0, HD, 512)]
    w2tiles = [(w2T[:, e, :, :], 32 * 128) for e in range(8)]
    wsc_ri, wsc_rap, wsc_rfree = ring.acquire()
    mt1 = meanpass(w1tiles, wsc_rap[0:1, 0:512], (True, True))
    mt2 = meanpass(w2tiles, wsc_rap[0:1, 512:1024], (True, True))

    xrdeps0 = B["xsrow"].begin_write()
    cw1 = sch.v(cpy(xs_row[:], wsc_rap[0:1, 0:512]), [mt1] + xrdeps0)
    hrdeps0 = B["hsrow"].begin_write()
    cw2 = sch.v(cpy(hs_row[:], wsc_rap[0:1, 512:1024]), [mt2] + hrdeps0)
    ring.release(wsc_ri, [cw1, cw2])
    t1_ = cw1
    t2_ = cw2
    n_ = 256
    while n_ >= 1:
        t1_ = sch.v(tt(xs_row[0:1, 0:n_], xs_row[0:1, 0:n_],
                       xs_row[0:1, n_:2 * n_], ALU.add), [t1_])
        t2_ = sch.v(tt(hs_row[0:1, 0:n_], hs_row[0:1, 0:n_],
                       hs_row[0:1, n_:2 * n_], ALU.add), [t2_])
        if n_ < 64:
            vgap()
        n_ //= 2
    B["xsrow"].wtok(t1_)
    B["hsrow"].wtok(t2_)
    e1 = sch.v(ts(wsc[:, 0:1], xs_row[0:1, 0:1], 1.0 / (D * HD), 1e-5,
                  ALU.mult, ALU.max), [t1_])
    e2 = sch.v(ts(wsc[:, 2:3], hs_row[0:1, 0:1], 1.0 / (D * HD), 1e-5,
                  ALU.mult, ALU.max), [t2_, e1])
    vgap()
    vgap()
    e3 = sch.v(recip(wsc[:, 1:2], wsc[:, 0:1]), [e2])
    vgap()
    vgap()
    e4 = sch.v(recip(wsc[:, 3:4], wsc[:, 2:3]), [e3])
    e4 = vgap()
    B["wsc"].wtok(e4)
    B["xsrow"].rtok(e1); B["hsrow"].rtok(e2)
    ri, rap, rfree = ring.acquire()
    pw = rap[:, 0:512]
    s1 = sch.t(mm(pw[:, 0:1], csb["ones1x128f"][:], wsc[:, 1:2], True, True),
               [e4] + rfree, rg=(0, 1))
    s2 = sch.t(mm(pw[:, 1:2], csb["ones1x128f"][:], wsc[:, 3:4], True, True),
               [e4], rg=(0, 1))
    e5 = sch.v(cpy(winvc1[:], pw[:, 0:1]), [s1, s2])
    e6 = sch.v(cpy(winvc2[:], pw[:, 1:2]), [s1, s2])
    ring.release(ri, [e5, e6])
    dbg_dump("d_wsc", wsc[:], [e4])

    def ternpass(tiles, winv_col, rg_name):
        for (src_ap, nel, dst_ap) in tiles:
            deps = B["scr"].begin_write()
            lt = sch.dma("SCR", dma(scrf[:, 0:nel].rearrange(
                "p (a o) -> p a o", a=src_ap.shape[1]), src_ap), deps)
            B["scr"].wtok(lt)
            i = wslot[0] % 2
            wslot[0] += 1
            wbuf = wbufs[i]
            bb = wbufB[i]
            wdeps = bb.begin_write()
            v1 = sch.v(ts(scrf[:, 0:nel], scrf[:, 0:nel], winv_col[:],
                          MAGIC, ALU.mult, ALU.add), [lt, e5, e6] + wdeps)
            v2 = sch.v(ts(scrf[:, 0:nel], scrf[:, 0:nel], -MAGIC, 1.0,
                          ALU.add, ALU.min), [v1])
            v3 = sch.v(ts(wbuf[:, 0:nel], scrf[:, 0:nel], -1.0, None,
                          ALU.max), [v2])
            B["scr"].wtok(v3)
            bb.wtok(v3)
            st = sch.dma("SS%d" % i, dma(dst_ap, wbuf[:, 0:nel].rearrange(
                "p (a o) -> p a o", a=dst_ap.shape[1])), [v3])
            bb.rtok(st)
            region_toks.setdefault(rg_name, []).append(st)

    sw1 = scw1.rearrange("(a p) o -> p a o", p=128)
    t1tiles = [(w1r[:, :, c0:c0 + 512], 8 * 512, sw1[:, :, c0:c0 + 512])
               for c0 in range(0, HD, 512)]
    t2tiles = [(w2T[:, e, :, :], 32 * 128, scw2[:, e, :, :]) for e in range(8)]
    ternpass(t1tiles, winvc1, "w1")
    ternpass(t2tiles, winvc2, "w2")

    WLOAD = []
    scqr = scq.rearrange("(a p) o -> p a o", p=128)
    sckr = sck.rearrange("(a p) o -> p a o", p=128)
    scvr = scv.rearrange("(a p) o -> p a o", p=128)
    scgr = scg.rearrange("(a p) o -> p a o", p=128)
    scabr = scab.rearrange("(a p) o -> p a o", p=128)
    scopr = scop.rearrange("(a p) o -> p a o", p=128)
    scour = scou.rearrange("(a p) o -> p a o", p=128)
    scw1r = scw1.rearrange("(a p) o -> p a o", p=128)
    for hf in range(2):
        WLOAD.append(("q%d" % hf, scqr[:, :, hf * 512:(hf + 1) * 512], 8, 512))
    for hf in range(2):
        WLOAD.append(("k%d" % hf, sckr[:, :, hf * 512:(hf + 1) * 512], 8, 512))
    for qf in range(4):
        WLOAD.append(("v%d" % qf, scvr[:, :, qf * 512:(qf + 1) * 512], 8, 512))
    for qf in range(4):
        WLOAD.append(("g%d" % qf, scgr[:, :, qf * 512:(qf + 1) * 512], 8, 512))
    WLOAD.append(("ab", scabr[:, :, :], 8, 64))
    for qf in range(4):
        WLOAD.append(("op%d" % qf, scopr[:, :, qf * 256:(qf + 1) * 256], 16, 256))
    for hf in range(2):
        WLOAD.append(("ou%d" % hf, scour[:, :, hf * 512:(hf + 1) * 512], 8, 512))
    for ef in range(8):
        WLOAD.append(("w1%d" % ef, scw1r[:, :, ef * 512:(ef + 1) * 512], 8, 512))
    for ef in range(8):
        WLOAD.append(("w2%d" % ef, scw2[:, ef, :, :], 32, 128))

    outr = outT.rearrange("(a p) t -> p a t", p=128)
    xr = xT.rearrange("(a p) t -> p a t", p=128)

    try:
      for seg in range(NSEG):
        _CURSEG[0] = seg
        _cut(0)
        tsl = slice(seg * TSEG, (seg + 1) * TSEG)

        deps = B["x0"].begin_write()
        xld = sch.dma("SX", dma(x0[:], xr[:, :, tsl]), deps)
        B["x0"].wtok(xld)

        wtoks = {}

        def wload(idx):
            key, src, a, o = WLOAD[idx]
            i = wslot[0] % 2
            wslot[0] += 1
            wbuf = wbufs[i]
            bb = wbufB[i]
            rg_name = key.rstrip("0123456789")
            deps = bb.begin_write() + region_toks.get(rg_name, [])
            view = wbuf.rearrange("p (a o) -> p a o", a=a)[:, :, 0:o]
            lt = sch.dma("SL%d" % i, dma(view, src), deps)
            bb.wtok(lt)
            wtoks[key] = (view, lt, bb)
            return view, lt, bb

        wload(0)
        wload(1)
        nload = [2]

        def wnext():
            if nload[0] < len(WLOAD):
                wload(nload[0])
                nload[0] += 1

        def wget(key):
            return wtoks[key]

        _cut(1) if False else None
        ri_s, rap_s, rfree_s = ring.acquire()
        psums = rap_s[:, 0:512]
        psumsq = rap_s[:, 512:1024]
        sq_toks = []
        mm_toks = []
        for cc in range(8):
            sqdep = B["tmp0" if cc % 2 == 0 else "tmp1"].begin_write()
            tb = tmp0 if cc % 2 == 0 else tmp1
            at = sch.a(act(tb[:], x0[:, cc, :], AF.Square), [xld] + sqdep)
            B["tmp0" if cc % 2 == 0 else "tmp1"].wtok(at)
            m1t = sch.t(mm(psums[0:1, :], csb["onescol128f"][:], x0[:, cc, :],
                           cc == 0, cc == 7), [xld] + (rfree_s if cc == 0 else []))
            m2t = sch.t(mm(psumsq[0:1, :], csb["onescol128f"][:], tb[:],
                           cc == 0, cc == 7), [at])
            B["tmp0" if cc % 2 == 0 else "tmp1"].rtok(m2t)
            mm_toks += [m1t, m2t]
        rv = sch.v(ts(m_row[:], psums[0:1, :], 1.0 / D, None, ALU.mult),
                   mm_toks)
        rv2 = sch.v(tt(var_row[:], m_row[:], m_row[:], ALU.mult), [rv])
        rv3 = sch.v(stt(var_row[:], psumsq[0:1, :], 1.0 / D, var_row[:],
                        ALU.mult, ALU.subtract), [rv2] + mm_toks)
        ring.release(ri_s, [rv3])
        ra = sch.a(act(r_row[:], var_row[:], AF.Sqrt, bias=eps5[0:1, :]), [rv3])
        rv4 = sch.v(recip(r_row[:], r_row[:]), [ra])
        rv4 = vgap()
        rv5 = sch.v(tt(r2_row[:], r_row[:], r_row[:], ALU.mult), [rv4])
        rv6 = sch.v(tt(r2_row[:], var_row[:], r2_row[:], ALU.mult), [rv5])
        ra2 = sch.a(act(r2_row[:], r2_row[:], AF.Sqrt, bias=eps6[0:1, :]), [rv6])
        rv7 = sch.v(recip(r2_row[:], r2_row[:]), [ra2])
        rv7 = vgap()
        rv8 = sch.v(tt(r_row[:], r_row[:], r2_row[:], ALU.mult), [rv7])
        rv9 = sch.v(tt(mr_row[:], m_row[:], r_row[:], ALU.mult), [rv8])
        xn_deps = B["xn"].begin_write()
        for cc in range(8):
            ri2, rap2, rfree2 = ring.acquire()
            pa = rap2[:, 0:512]
            pb = rap2[:, 512:1024]
            b1t = sch.t(mm(pa, nwr_s[:, cc * 128:(cc + 1) * 128], r_row[:],
                           True, True), [rv8, ldtok] + rfree2, rg=(0, 1))
            b2t = sch.t(mm(pb, nwr_s[:, cc * 128:(cc + 1) * 128], mr_row[:],
                           True, True), [rv9], rg=(0, 1))
            tb = tmp0 if cc % 2 == 0 else tmp1
            tdep = B["tmp0" if cc % 2 == 0 else "tmp1"].begin_write()
            v1 = sch.v(tt(tb[:], x0[:, cc, :], pa, ALU.mult),
                       [b1t, xld] + tdep)
            B["tmp0" if cc % 2 == 0 else "tmp1"].wtok(v1)
            v2 = sch.v(tt(xn[:, cc, :], tb[:], pb, ALU.subtract),
                       [v1, b2t] + xn_deps)
            B["tmp0" if cc % 2 == 0 else "tmp1"].rtok(v2)
            B["xn"].wtok(v2)
            ring.release(ri2, [v2])
        _cut(1)
        if seg == DBGSEG:
            dbg_dump("d_xn", xn[:], B["xn"].rd(), B["xn"])

        if seg == DBGSEG and DBG and "d_hal" in dbg_outs:
            dbg_dump("d_hal", halo[:], B["halo"].rd(), B["halo"])
        def conv_chunk(chg, dst_ap, dstB, psum_ap, psum_toks, ring_rel):
            pp = chg % 2
            qp = qpre[:, pp, :]
            qB = B["qpre%d" % pp]
            qdeps = qB.begin_write()
            hv = sch.v(cpy(qp[:, 0:3], halo[:, chg, :]),
                       qdeps + B["halo"].rd())
            qB.wtok(hv)
            ev = sch.v(cpy(qp[:, 3:3 + TSEG], psum_ap), psum_toks)
            qB.wtok(ev)
            hdep = B["halo"].begin_write()
            hv2 = sch.v(cpy(halo[:, chg, :], psum_ap[:, TSEG - 3:TSEG]),
                        psum_toks + hdep)
            B["halo"].wtok(hv2)
            ring_rel([ev, hv2])
            dp = 0
            dB = B["diag0"]
            ddeps = dB.begin_write()
            dts = []
            for tap in range(4):
                dt_ = sch.v(ts(diag[:, dp, tap, :], csb["I128b"][:],
                               cw_s[:, chg, tap:tap + 1], None, ALU.mult),
                            ddeps + consts_tok)
                dB.wtok(dt_)
                dts.append(dt_)
            ri3, rap3, rfree3 = ring.acquire()
            pc = rap3[:, 0:512]
            mt = None
            for tap in range(4):
                mt = sch.t(mm(pc, diag[:, dp, tap, :], qp[:, tap:tap + TSEG],
                              tap == 0, tap == 3),
                           [dts[tap], ev, hv] + (rfree3 if tap == 0 else []))
            dB.rtok(mt)
            qB.rtok(mt)
            st_ = sch.a(act(dst_ap, pc, AF.Silu), [mt])
            dstB.wtok(st_)
            ring.release(ri3, [st_])
            return st_

        def proj_group(wkey, mlist, dst_kind):
            view, lt, bb = wget(wkey)
            for (m_rel, info) in mlist:
                ri4, rap4, rfree4 = ring.acquire()
                pp = rap4[:, 0:512]
                mt = None
                for kc in range(8):
                    mt = sch.t(mm(pp, view[:, kc, m_rel * 128:(m_rel + 1) * 128],
                                  xn[:, kc, :], kc == 0, kc == 7),
                               [lt] + B["xn"].rd() + (rfree4 if kc == 0 else []))
                bb.rtok(mt)

                def rel(toks, ri4=ri4):
                    ring.release(ri4, toks)
                kind = info[0]
                if kind == "conv":
                    _, dstap, dstB, chg = info
                    conv_chunk(chg, dstap, dstB, pp, [mt], rel)
                else:
                    _, dstap, dstB = info
                    et = sch.a(acpy(dstap, pp), [mt])
                    dstB.wtok(et)
                    rel([et])
            wnext()

        qdeps = B["qb"].begin_write()
        kdeps = B["kb"].begin_write()
        vdeps = B["vb"].begin_write()
        gdeps = B["gate"].begin_write()
        for hf in range(2):
            proj_group("q%d" % hf,
                       [(m, ("conv", qb[:, hf * 4 + m, :], B["qb"],
                             hf * 4 + m)) for m in range(4)], None)
        for hf in range(2):
            proj_group("k%d" % hf,
                       [(m, ("conv", kb[:, hf * 4 + m, :], B["kb"],
                             8 + hf * 4 + m)) for m in range(4)], None)
        for qf in range(4):
            proj_group("v%d" % qf,
                       [(m, ("conv", vb[:, qf * 4 + m, :], B["vb"],
                             16 + qf * 4 + m)) for m in range(4)], None)
        for qf in range(4):
            proj_group("g%d" % qf,
                       [(m, ("copy", gate[:, qf * 4 + m, :], B["gate"]))
                        for m in range(4)], None)
        view, lt, bb = wget("ab")
        ri5, rap5, rfree5 = ring.acquire()
        pz = rap5[:, 0:512]
        mt = None
        for kc in range(8):
            mt = sch.t(mm(pz[0:64, :], view[:, kc, 0:64], xn[:, kc, :],
                          kc == 0, kc == 7),
                       [lt] + B["xn"].rd() + (rfree5 if kc == 0 else []))
        bb.rtok(mt)
        zdeps = B["zg"].begin_write()
        zt = sch.v(cpy(zg[:], pz[0:64, :]), [mt] + zdeps)
        B["zg"].wtok(zt)
        ring.release(ri5, [zt])
        wnext()
        _cut(2)
        if seg == DBGSEG:
            dbg_dump("d_zg", zg[:], B["zg"].rd(), B["zg"])
            dbg_dump("d_gate", gate[:], B["gate"].rd(), B["gate"])
            dbg_dump("d_hal2", halo[:], B["halo"].rd(), B["halo"])

        za = zg[0:16, :]
        zb = zg[32:48, :]
        r1deps = B["rt1"].begin_write()
        r2deps = B["rt2"].begin_write()
        a1 = sch.a(act(rt1[:], za, AF.Abs, bias=dtb_s[:]),
                   B["zg"].rd() + [ldtok] + r1deps)
        a2 = sch.a(act(rt1[:], rt1[:], AF.Exp, scale=-1.0), [a1])
        a3 = sch.a(act(rt1[:], rt1[:], AF.Ln, bias=one1[0:16, :]),
                   [a2, to1])
        a4 = sch.a(act(rt2[:], za, AF.Relu, bias=dtb_s[:]),
                   B["zg"].rd() + r2deps)
        gdeps2 = B["g16"].begin_write()
        g1 = sch.v(tt(g16[:], rt1[:], rt2[:], ALU.add), [a3, a4] + gdeps2)
        g2 = sch.v(ts(g16[:], g16[:], negA[:], None, ALU.mult),
                   [g1, negA_t])
        B["g16"].wtok(g2)
        b1t = sch.a(act(rt1[:], zb, AF.Abs), [g1] + B["zg"].rd())
        b2t = sch.a(act(rt1[:], rt1[:], AF.Exp, scale=-1.0), [b1t])
        b3t = sch.a(act(rt1[:], rt1[:], AF.Ln, bias=one1[0:16, :]),
                    [b2t, to1])
        b4t = sch.a(act(rt2[:], zb, AF.Relu, scale=-1.0),
                    [g1] + B["zg"].rd())
        lbdeps = B["lnb16"].begin_write()
        lb = sch.v(stt(lnb16[:], rt2[:], -1.0, rt1[:], ALU.mult,
                       ALU.subtract), [b3t, b4t] + lbdeps)
        B["lnb16"].wtok(lb)
        B["rt1"].wtok(b3t); B["rt2"].wtok(b4t)
        cumdeps = B["cum16"].begin_write()
        ctoks = []
        for c in range(NCH):
            csl = slice(c * C, (c + 1) * C)
            ct = sch.v(scan_op(cum16[:, csl], ones16[:, csl], g16[:, csl],
                               0.0), [g2, t3] + cumdeps)
            B["cum16"].wtok(ct)
            ctoks.append(ct)
        ctf = vgap()
        B["cum16"].wtok(ctf)
        ctoks = [ctf]
        mldeps = B["mln16"].begin_write()
        ml = sch.v(stt(mln16[:], cum16[:], -1.0, lnb16[:], ALU.mult,
                       ALU.add), ctoks + [lb] + mldeps)
        B["mln16"].wtok(ml)
        if seg == DBGSEG and DBG and "d_rows" in dbg_outs:
            _deps = (B["g16"].rd() + B["lnb16"].rd() + B["cum16"].rd() +
                     B["mln16"].rd())
            for _i, _rt in enumerate([g16, lnb16, cum16, mln16]):
                dbg_dump_slot("d_rows", _i, _rt[:], _deps)

        _cut(3)
        ri6, rap6, rfree6 = ring.acquire()
        pk = rap6[:, 0:512]
        mt = None
        for cc in range(8):
            mt = sch.t(mm(pk[0:16, :], csb["bo16b"][:, cc, :], kb[:, cc, :],
                          cc == 0, cc == 7),
                       B["kb"].rd() + consts_tok + (rfree6 if cc == 0 else []))
        kmdeps = B["kmean"].begin_write()
        km = sch.v(ts(kmean_b[:], pk[0:16, :], 1.0 / 64, None, ALU.mult),
                   [mt] + kmdeps)
        B["kmean"].wtok(km)
        ring.release(ri6, [km])
        kdeps2 = B["kb"].begin_write()
        kc_toks = []
        for cc in range(8):
            ri7, rap7, rfree7 = ring.acquire()
            pb2 = rap7[:, 0:512]
            bt = sch.t(mm(pb2, csb["bc16b"][:, cc, :], kmean_b[:], True, True),
                       [km] + rfree7, rg=(0, 16))
            vt = sch.v(tt(kb[:, cc, :], kb[:, cc, :], pb2, ALU.subtract),
                       [bt] + kdeps2)
            B["kb"].wtok(vt)
            kc_toks.append(vt)
            ring.release(ri7, [vt])

        def sumsq_scale(src, srcB, dst_b, dstB, extra_mul):
            ri8, rap8, rfree8 = ring.acquire()
            pq = rap8[:, 0:512]
            mt = None
            for cc in range(8):
                sq = sqpp[:, cc % 2, :]
                sqB = B["sq%d" % (cc % 2)]
                sdeps = sqB.begin_write()
                st_ = sch.v(tt(sq, src[:, cc, :], src[:, cc, :], ALU.mult),
                            srcB.rd() + sdeps)
                sqB.wtok(st_)
                mt = sch.t(mm(pq[0:16, :], csb["bo16b"][:, cc, :], sq,
                              cc == 0, cc == 7),
                           [st_] + (rfree8 if cc == 0 else []))
                sqB.rtok(mt)
            rdeps = B["rt1"].begin_write()
            at = sch.a(act(rt1[:], pq[0:16, :], AF.Sqrt, bias=eps6[0:16, :]), [mt] + rdeps)
            ring.release(ri8, [at])
            vt = sch.v(recip(rt1[:], rt1[:]), [at])
            vt = vgap()
            B["rt1"].wtok(vt)
            ddeps = dstB.begin_write()
            if extra_mul is None:
                ct = sch.v(cpy(dst_b[:], rt1[:]), [vt] + ddeps)
            else:
                ct = sch.v(ts(dst_b[:], rt1[:], extra_mul, None, ALU.mult),
                           [vt] + ddeps)
            dstB.wtok(ct)
            B["rt1"].rtok(ct)
            ndeps = srcB.begin_write()
            for cc in range(8):
                ri9, rap9, rfree9 = ring.acquire()
                pb3 = rap9[:, 0:512]
                bt = sch.t(mm(pb3, csb["bc16b"][:, cc, :], dst_b[:],
                              True, True), [ct] + rfree9, rg=(0, 16))
                vt2 = sch.v(tt(src[:, cc, :], src[:, cc, :], pb3, ALU.mult),
                            [bt] + ndeps)
                srcB.wtok(vt2)
                ring.release(ri9, [vt2])

        sumsq_scale(kb, B["kb"], kmean_b, B["ksc"], None)
        sumsq_scale(qb, B["qb"], qsc_b, B["qsc"], 0.125)
        if seg == DBGSEG:
            dbg_dump("d_q", qb[:], B["qb"].rd(), B["qb"])
            dbg_dump("d_k", kb[:], B["kb"].rd(), B["kb"])
            dbg_dump("d_v", vb[:], B["vb"].rd(), B["vb"])

        _cut(4)
        og_toks = []
        for c in range(NCH):
            csl = slice(c * C, (c + 1) * C)
            cend = c * C + C - 1
            wdeps = B["wtr16"].begin_write()
            w1_ = sch.v(stt(wtr16[:, :], cum16[:, csl],
                            cum16[:, cend:cend + 1], lnb16[:, csl],
                            ALU.subtract, ALU.subtract),
                        B["cum16"].rd() + B["lnb16"].rd() + wdeps)
            w2_ = sch.a(act(wtr16[:, :], wtr16[:, :], AF.Exp, scale=-1.0),
                        [w1_])
            B["wtr16"].wtok(w2_)
            wbdeps = B["wtrb"].begin_write()
            w3_ = sch.v(cpy(wtrow_b[:], wtr16[:, :]), [w2_] + wbdeps)
            B["wtrb"].wtok(w3_)
            ri10, rap10, rfree10 = ring.acquire()
            ri10b, rap10b, rfree10b = ring.acquire()
            l1 = sch.t(mm(rap10[0:1, 0:16], cum16[:, cend:cend + 1],
                          csb["I16f"][:], True, True),
                       B["cum16"].rd() + rfree10 + consts_tok, rg=(0, 16))
            lcdeps = B["lcrow"].begin_write()
            l2 = sch.v(cpy(lcrow[:], rap10[0:1, 0:16]), [l1] + lcdeps)
            l3 = sch.a(act(lcrow[:], lcrow[:], AF.Exp), [l2])
            B["lcrow"].wtok(l3)
            l4 = sch.t(mm(rap10[:, 512:528], csb["ones1x128f"][:], lcrow[:],
                          True, True), [l3], rg=(0, 1))
            lbdeps2 = B["LCb"].begin_write()
            l5 = sch.v(cpy(LCb[:], rap10[:, 512:528]), [l4] + lbdeps2)
            B["LCb"].wtok(l5)
            c1 = sch.t(mm(rap10b[:, 0:16], cum16[:, csl], csb["I16f"][:],
                          True, True), B["cum16"].rd() + rfree10b, rg=(0, 16))
            c2 = sch.t(mm(rap10b[:, 512:528], mln16[:, csl], csb["I16f"][:],
                          True, True), B["mln16"].rd(), rg=(0, 16))
            ctdeps = B["ctT"].begin_write()
            c3 = sch.v(cpy(ctT[:, 0:16], rap10b[:, 0:16]), [c1, c2] + ctdeps)
            c3 = sch.v(cpy(ctT[:, 16:32], rap10b[:, 512:528]), [c3])
            vgap()
            c3 = vgap()
            B["ctT"].wtok(c3)
            ring.release(ri10, [l5])
            ring.release(ri10b, [c3])
            lamdeps = B["lamb"].begin_write()
            lm = sch.a(act(lamb[:], cum16[:, csl], AF.Exp),
                       B["cum16"].rd() + lamdeps)
            B["lamb"].wtok(lm)

            if c == 0:
                _cut(41)
            edeps = B["Et"].begin_write()
            e_toks = []
            for hb in range(4):
                ri11, rap11, rfree11 = ring.acquire()
                ve = rap11.rearrange("p (a o) -> p a o", a=4)
                mts = []
                for hhh in range(4):
                    h = hb * 4 + hhh
                    m1_ = sch.t(mm(ve[:, hhh, 0:128], csb["sel16f"][:, h, :],
                                   cum16[:, csl], True, True),
                                B["cum16"].rd() + consts_tok +
                                (rfree11 if hhh == 0 else []), rg=(0, 16))
                    m2_ = sch.t(mm(ve[:, hhh, 128:256], csb["sel16f"][:, h, :],
                                   mln16[:, csl], True, True),
                                B["mln16"].rd(), rg=(0, 16))
                    mts += [m1_, m2_]
                stts = []
                for hhh in range(4):
                    h = hb * 4 + hhh
                    s1_ = sch.v(stt(ve[:, hhh, 0:128], ve[:, hhh, 0:128],
                                    ctT[:, 16 + h:17 + h], csb["mup"][:],
                                    ALU.add, ALU.add), mts + [c3])
                    s2_ = sch.v(stt(ve[:, hhh, 128:256], ve[:, hhh, 128:256],
                                    ctT[:, h:h + 1], csb["mdn"][:],
                                    ALU.add, ALU.add), mts + [c3])
                    stts += [s1_, s2_]
                ex = sch.a(act(Et[:, hb * 4:hb * 4 + 4, :], ve[:, :, :],
                               AF.Exp), stts + edeps)
                B["Et"].wtok(ex)
                e_toks.append(ex)
                ring.release(ri11, [ex])
            esdeps = B["Es"].begin_write()
            est = None
            for hb in range(4):
                hsl = slice(hb * 4, hb * 4 + 4)
                est = sch.v(tt(Es[:, hsl, :], Et[:, hsl, 0:128],
                               csb["s01u"][:], ALU.mult),
                            e_toks + esdeps + consts_tok)
                B["Es"].wtok(est)

            if c == 0:
                _cut(42)
            if seg == DBGSEG and c == 0:
                dbg_dump("d_Et", Et[:], e_toks, B["Et"])
                dbg_dump("d_Es", Es[:], B["Es"].rd(), B["Es"])
            g1deps = B["Gc0"].begin_write()
            h1deps = B["Hc0"].begin_write()
            atdeps = B["At"].begin_write()
            for hb in range(4):
                ri12, rap12, rfree12 = ring.acquire()
                vk = rap12.rearrange("p (a o) -> p a o", a=4)
                mts = []
                for hhh in range(4):
                    h = hb * 4 + hhh
                    bh = (h % 2) * 64
                    qc = h // 2
                    ksl = kb[bh:bh + 64, qc, csl]
                    qsl = qb[bh:bh + 64, qc, csl]
                    mk = sch.t(mm(vk[:, hhh, 0:128], ksl, ksl, True, True,
                                  tp=(bh, 0)),
                               B["kb"].rd() + (rfree12 if hhh == 0 else []),
                               rg=(bh, 64))
                    mq = sch.t(mm(vk[:, hhh, 128:256], ksl, qsl, True, True,
                                  tp=(bh, 0)), B["qb"].rd(), rg=(bh, 64))
                    mts += [mk, mq]
                hsl = slice(hb * 4, hb * 4 + 4)
                kkv = vk[:, :, 0:128]
                kqv = vk[:, :, 128:256]
                gv = sch.v(stt(Gc[:, 0, hsl, :], kkv, -1.0, Es[:, hsl, :],
                               ALU.mult, ALU.mult), mts + [est] + g1deps)
                B["Gc0"].wtok(gv)
                hv = sch.v(stt(Hc[:, 0, hsl, :], kkv, -1.0,
                               Et[:, hsl, 128:256], ALU.mult, ALU.mult),
                           mts + e_toks + h1deps)
                B["Hc0"].wtok(hv)
                av = sch.v(tt(At[:, hsl, :], kqv, Et[:, hsl, 0:128],
                              ALU.mult), mts + e_toks + atdeps)
                B["At"].wtok(av)
                ring.release(ri12, [gv, hv, av])

            if c == 0:
                _cut(43)
            if seg == DBGSEG and c == 0:
                dbg_dump("d_G1", Gc[:, 0, :, :], B["Gc0"].rd(), B["Gc0"])
                dbg_dump("d_H1", Hc[:, 0, :, :], B["Hc0"].rd(), B["Hc0"])
                dbg_dump("d_At", At[:], B["At"].rd(), B["At"])
            ri13, rap13, rfree13 = ring.acquire()
            vt13 = rap13.rearrange("p (a o) -> p a o", a=16)
            mt = None
            first = True
            for h in range(16):
                bh = (h % 2) * 64
                qc = h // 2
                mt = sch.t(mm(vt13[:, h, :], kb[bh:bh + 64, qc, csl],
                              csb["I64dualb"][bh:bh + 64, :], True, True,
                              tp=(bh, 0)),
                           B["kb"].rd() + consts_tok +
                           (rfree13 if first else []), rg=(bh, 64))
                first = False
            ri14, rap14, rfree14 = ring.acquire()
            b1_ = sch.t(mm(rap14[:, 0:512], wtrow_b[:], csb["bd64b"][:, 0:512],
                           True, True), [w3_] + consts_tok + rfree14,
                        rg=(0, 16))
            b2_ = sch.t(mm(rap14[:, 512:1024], wtrow_b[:],
                           csb["bd64b"][:, 512:1024], True, True), [w3_],
                        rg=(0, 16))
            wbcdeps = B["wbc"].begin_write()
            wb_ = sch.a(acpy(wbc.rearrange("p a o -> p (a o)"), rap14[:, :]),
                        [b1_, b2_] + wbcdeps)
            B["wbc"].wtok(wb_)
            ring.release(ri14, [wb_])
            ktdeps = B["kTW"].begin_write()
            kt_ = sch.v(tt(kTW.rearrange("p a o -> p (a o)"), vt13.rearrange(
                "p a o -> p (a o)"), wbc.rearrange("p a o -> p (a o)"),
                ALU.mult), [mt, wb_] + ktdeps)
            B["kTW"].wtok(kt_)
            ring.release(ri13, [kt_])

            if c == 0:
                _cut(44)
            ri15, rap15, rfree15 = ring.acquire()
            vq = rap15.rearrange("p (a o) -> p a o", a=8)
            mt = None
            for cc in range(8):
                mt = sch.t(mm(vq[:, cc, :], csb["bc16b"][:, cc, :], lamb[:],
                              True, True),
                           [lm] + consts_tok + (rfree15 if cc == 0 else []),
                           rg=(0, 16))
            qldeps = B["qL"].begin_write()
            qv3 = qb[:, :, csl]
            kv3 = kb[:, :, csl]
            ql_ = sch.v(tt(qL[:, :, :], qv3, vq[:, :, :], ALU.mult),
                        [mt] + B["qb"].rd() + qldeps)
            B["qL"].wtok(ql_)
            nkdeps = B["nkL"].begin_write()
            nk_ = sch.v(stt(nkL[:, :, :], kv3, -1.0, vq[:, :, :],
                            ALU.mult, ALU.mult),
                        [mt] + B["kb"].rd() + nkdeps)
            B["nkL"].wtok(nk_)
            ring.release(ri15, [ql_, nk_])

            if c == 0:
                _cut(45)
            tmdeps = B["Tm"].begin_write()
            for hb in range(2):
                ri16, rap16, rfree16 = ring.acquire()
                vy = rap16.rearrange("p (a o) -> p a o", a=8)
                mts = []
                for hhh in range(8):
                    h = hb * 8 + hhh
                    bh = (h % 2) * 64
                    qc = h // 2
                    my1 = sch.t(mm(vy[:, hhh, :], vb[:, h, csl],
                                   csb["I128b"][:], True, False),
                                B["vb"].rd() + consts_tok +
                                (rfree16 if hhh == 0 else []))
                    my2 = sch.t(mm(vy[:, hhh, :], nkL[bh:bh + 64, qc, :],
                                   S_b[bh:bh + 64, qc, :], False, True,
                                   tp=(bh, 0)),
                                [nk_] + B["S_b"].rd(), rg=(bh, 64))
                    mts += [my1, my2]
                tv = sch.a(acpy(Tm[:, hb * 8:hb * 8 + 8, :], vy[:, :, :]),
                           mts + tmdeps)
                B["Tm"].wtok(tv)
                ring.release(ri16, [tv])

            if c == 0:
                _cut(46)
            if seg == DBGSEG and c == 0:
                dbg_dump("d_Tm0", Tm[:], B["Tm"].rd(), B["Tm"])
                dbg_dump("d_kTW", kTW[:], B["kTW"].rd(), B["kTW"])
                dbg_dump("d_qL", qL[:], B["qL"].rd(), B["qL"])
                dbg_dump("d_nkL", nkL[:], B["nkL"].rd(), B["nkL"])
            for li, pp_ in enumerate([1, 2, 4, 8, 16, 32, 64]):
                cur = li % 2
                nxt = 1 - cur
                Gcur = Gc[:, cur, :, :]
                Hcur = Hc[:, cur, :, :]
                GB = B["Gc%d" % cur]
                HB = B["Hc%d" % cur]
                tmdeps2 = B["Tm"].begin_write()
                for hb in range(2):
                    ri17, rap17, rfree17 = ring.acquire()
                    va = rap17.rearrange("p (a o) -> p a o", a=8)
                    mts = []
                    for hhh in range(8):
                        h = hb * 8 + hhh
                        ma = sch.t(mm(va[:, hhh, :], Gcur[:, h, :],
                                      Tm[:, h, :], True, True),
                                   GB.rd() + tmdeps2 +
                                   (rfree17 if hhh == 0 else []))
                        mts.append(ma)
                    GB.rtok(mts[-1])
                    uv = sch.v(tt(Tm[:, hb * 8:hb * 8 + 8, :], va[:, :, :],
                                  Tm[:, hb * 8:hb * 8 + 8, :], ALU.add),
                               mts)
                    B["Tm"].wtok(uv)
                    ring.release(ri17, [uv])
                if pp_ < 64:
                    GnB = B["Gc%d" % nxt]
                    HnB = B["Hc%d" % nxt]
                    gn_deps = GnB.begin_write()
                    hn_deps = HnB.begin_write() if pp_ < 32 else None
                    last_sq = None
                    for hb in range(2):
                        hsl = slice(hb * 8, hb * 8 + 8)
                        ri18, rap18, rfree18 = ring.acquire()
                        vg = rap18.rearrange("p (a o) -> p a o", a=8)
                        mts = []
                        for hhh in range(8):
                            h = hb * 8 + hhh
                            mg = sch.t(mm(vg[:, hhh, :], Hcur[:, h, :],
                                          Gcur[:, h, :], True, True),
                                       GB.rd() + HB.rd() +
                                       (rfree18 if hhh == 0 else []))
                            mts.append(mg)
                        last_sq = mts[-1]
                        gv2 = sch.a(acpy(Gc[:, nxt, hsl, :], vg[:, :, :]),
                                    mts + gn_deps)
                        GnB.wtok(gv2)
                        ring.release(ri18, [gv2])
                        if pp_ < 32:
                            ri19, rap19, rfree19 = ring.acquire()
                            vh = rap19.rearrange("p (a o) -> p a o", a=8)
                            mts2 = []
                            for hhh in range(8):
                                h = hb * 8 + hhh
                                mh = sch.t(mm(vh[:, hhh, :], Gcur[:, h, :],
                                              Hcur[:, h, :], True, True),
                                           GB.rd() + HB.rd() +
                                           (rfree19 if hhh == 0 else []))
                                mts2.append(mh)
                            last_sq = mts2[-1]
                            hv2 = sch.a(acpy(Hc[:, nxt, hsl, :], vh[:, :, :]),
                                        mts2 + hn_deps)
                            HnB.wtok(hv2)
                            ring.release(ri19, [hv2])
                    GB.rtok(last_sq)
                    HB.rtok(last_sq)

            if c == 0:
                _cut(47)
            if seg == DBGSEG and c == 0:
                dbg_dump("d_Tm", Tm[:], B["Tm"].rd(), B["Tm"])
            osbdeps = B["Osb"].begin_write()
            for hb in range(2):
                ri20, rap20, rfree20 = ring.acquire()
                vo = rap20.rearrange("p (a o) -> p a o", a=8)
                mts = []
                for hhh in range(8):
                    h = hb * 8 + hhh
                    bh = (h % 2) * 64
                    qc = h // 2
                    mo1 = sch.t(mm(vo[:, hhh, :], At[:, h, :], Tm[:, h, :],
                                   True, False),
                                B["At"].rd() + B["Tm"].rd() +
                                (rfree20 if hhh == 0 else []))
                    mo2 = sch.t(mm(vo[:, hhh, :], qL[bh:bh + 64, qc, :],
                                   S_b[bh:bh + 64, qc, :], False, True,
                                   tp=(bh, 0)), [ql_] + B["S_b"].rd(),
                                rg=(bh, 64))
                    mts += [mo1, mo2]
                ov = sch.a(acpy(Osb[:, hb * 8:hb * 8 + 8, :], vo[:, :, :]),
                           mts + osbdeps)
                B["Osb"].wtok(ov)
                ring.release(ri20, [ov])
            if c == 0:
                _cut(47.5)
            if seg == DBGSEG and c == 0:
                dbg_dump("d_Osb", Osb[:], B["Osb"].rd(), B["Osb"])
                dbg_dump("d_Sb0", S_b[:], B["S_b"].rd(), B["S_b"])
            ochdeps = B["och"].begin_write()
            for hb in range(2):
                ri21, rap21, rfree21 = ring.acquire()
                vf = rap21.rearrange("p (a o) -> p a o", a=8)
                mts = []
                for hhh in range(8):
                    h = hb * 8 + hhh
                    mf = sch.t(mm(vf[:, hhh, :], Osb[:, h, :], csb["I128b"][:],
                                  True, True),
                               B["Osb"].rd() + (rfree21 if hhh == 0 else []))
                    mts.append(mf)
                fv = sch.a(acpy(och[:, hb * 8:hb * 8 + 8, :], vf[:, :, :]),
                           mts + ochdeps)
                B["och"].wtok(fv)
                ring.release(ri21, [fv])
            if seg == DBGSEG and c == 0:
                dbg_dump("d_och", och[:], B["och"].rd(), B["och"])

            if c == 0:
                _cut(48)
            ildeps = B["IL"].begin_write()
            il_read = {}
            ri22, rap22, rfree22 = ring.acquire()
            vs = rap22.rearrange("p (a o) -> p a o", a=8)
            s_mts = []
            for h in range(16):
                bh = (h % 2) * 64
                qc = h // 2
                dep = ([il_read[h - 2]] if (h - 2) in il_read else ildeps)
                ilv = sch.v(ts(IL[bh:bh + 64, 0, :],
                               csb["I64dualf"][bh:bh + 64, :],
                               LCb[bh:bh + 64, h:h + 1], None, ALU.mult),
                            dep + [l5] + consts_tok)
                B["IL"].wtok(ilv)
                ms1 = sch.t(mm(vs[bh:bh + 64, qc, :], kTW[:, h, :],
                               Tm[:, h, :], True, False, tp=(0, bh)),
                            [kt_] + B["Tm"].rd() +
                            (rfree22 if h == 0 else []))
                ms2 = sch.t(mm(vs[bh:bh + 64, qc, :], IL[bh:bh + 64, 0, :],
                               S_f[bh:bh + 64, qc, :], False, True,
                               tp=(bh, bh)),
                            [ilv] + B["S_f"].rd(), rg=(bh, 64))
                il_read[h] = ms2
                s_mts += [ms1, ms2]
                B["IL"].rtok(ms2)
                if c == 0 and h == 0:
                    _cut(48.2)
                if c == 0 and h == 7:
                    _cut(48.5)
            if c == 0:
                _cut(48.8)
            sfdeps = B["S_f"].begin_write()
            sf_ = sch.v(cpy(S_f.rearrange("p a o -> p (a o)"),
                            vs.rearrange("p a o -> p (a o)")), s_mts + sfdeps)
            B["S_f"].wtok(sf_)
            sbdeps = B["S_b"].begin_write()
            sb_ = sch.a(acpy(S_b.rearrange("p a o -> p (a o)"),
                             S_f.rearrange("p a o -> p (a o)")),
                        [sf_] + sbdeps)
            B["S_b"].wtok(sb_)
            B["S_f"].rtok(sb_)
            ring.release(ri22, [sf_])

            if c == 0:
                _cut(49)
            osgdeps = B["osg"].begin_write()
            oq = sch.v(tt(osg.rearrange("p a o -> p (a o)"),
                          och.rearrange("p a o -> p (a o)"),
                          och.rearrange("p a o -> p (a o)"), ALU.mult),
                       B["och"].rd() + osgdeps)
            B["osg"].wtok(oq)
            ri23, rap23, rfree23 = ring.acquire()
            po = rap23[:, 0:512]
            mt = None
            for h in range(16):
                mt = sch.t(mm(po[0:16, 0:128], csb["oc16b"][:, h, :],
                              osg[:, h, :], h == 0, h == 15),
                           [oq] + consts_tok + (rfree23 if h == 0 else []))
            B["osg"].rtok(mt)
            rdeps3 = B["rt1"].begin_write()
            ot = sch.a(act(rt1[:, 0:128], po[0:16, 0:128], AF.Sqrt,
                           bias=eps5[0:16, :], scale=1.0 / DV), [mt] + rdeps3)
            ring.release(ri23, [ot])
            ov2 = sch.v(recip(rt1[:, 0:128], rt1[:, 0:128]), [ot])
            ov2 = vgap()
            B["rt1"].wtok(ov2)
            oc_ = ov2
            osgdeps2 = B["osg"].begin_write()
            gs = sch.a(act(osg[:, :, :], gate[:, :, csl], AF.Silu),
                       B["gate"].rd() + osgdeps2)
            B["osg"].wtok(gs)
            gdeps3 = B["gate"].begin_write()
            och_w2 = B["och"].begin_write()
            for hb in range(2):
                hsl = slice(hb * 8, hb * 8 + 8)
                ri24, rap24, rfree24 = ring.acquire()
                vb2 = rap24.rearrange("p (a o) -> p a o", a=8)
                mts = []
                for hhh in range(8):
                    h = hb * 8 + hhh
                    mb = sch.t(mm(vb2[:, hhh, :], csb["sel16f"][:, h, :],
                                  rt1[:, 0:128], True, True),
                               [oc_] + consts_tok +
                               (rfree24 if hhh == 0 else []), rg=(0, 16))
                    mts.append(mb)
                B["rt1"].rtok(mts[-1])
                o1_ = sch.v(stt(och[:, hsl, :], och[:, hsl, :], onw_s[:],
                                vb2[:, :, :], ALU.mult, ALU.mult),
                            mts + och_w2 + [ldtok, oq, gs])
                B["och"].wtok(o1_)
                o2_ = sch.v(tt(gate[:, hsl, csl], och[:, hsl, :],
                               osg[:, hsl, :], ALU.mult),
                            [o1_, gs] + gdeps3)
                B["gate"].wtok(o2_)
                og_toks.append(o2_)
                ring.release(ri24, [o2_])
            if c == 0:
                _cut(50)

        _cut(60)
        if seg == DBGSEG:
            dbg_dump("d_og", gate[:], B["gate"].rd(), B["gate"])
            dbg_dump("d_sf", S_f[:], B["S_f"].rd(), B["S_f"])

        m1deps = B["m1"].begin_write()
        for qf in range(4):
            view, lt, bb = wget("op%d" % qf)
            for mr in range(2):
                mc = qf * 2 + mr
                ri25, rap25, rfree25 = ring.acquire()
                pp3 = rap25[:, 0:512]
                mt = None
                for kc in range(16):
                    mt = sch.t(mm(pp3, view[:, kc, mr * 128:(mr + 1) * 128],
                                  gate[:, kc, :], kc == 0, kc == 15),
                               [lt] + og_toks +
                               (rfree25 if kc == 0 else []))
                bb.rtok(mt)
                ev = sch.a(acpy(m1[:, mc, :], pp3), [mt] + m1deps)
                B["m1"].wtok(ev)
                ring.release(ri25, [ev])
            wnext()
        if seg == DBGSEG:
            dbg_dump("d_m1", m1[:], B["m1"].rd(), B["m1"])
        scrdeps = B["scr"].begin_write()
        for hf in range(2):
            view, lt, bb = wget("ou%d" % hf)
            for mr in range(4):
                mc = hf * 4 + mr
                ri26, rap26, rfree26 = ring.acquire()
                pp4 = rap26[:, 0:512]
                mt = None
                for kc in range(8):
                    mt = sch.t(mm(pp4, view[:, kc, mr * 128:(mr + 1) * 128],
                                  m1[:, kc, :], kc == 0, kc == 7),
                               [lt] + B["m1"].rd() +
                               (rfree26 if kc == 0 else []))
                bb.rtok(mt)
                ev = sch.v(cpy(scr[:, mc, :], pp4), [mt] + scrdeps)
                B["scr"].wtok(ev)
                ring.release(ri26, [ev])
            wnext()

        _cut(70)
        def ln_stats(srcbuf, srcB):
            ri_l, rap_l, rfree_l = ring.acquire()
            psu = rap_l[:, 0:512]
            psq = rap_l[:, 512:1024]
            mm_t = []
            for cc in range(8):
                tb = tmp0 if cc % 2 == 0 else tmp1
                tB = B["tmp0" if cc % 2 == 0 else "tmp1"]
                tdeps = tB.begin_write()
                at = sch.a(act(tb[:], srcbuf[:, cc, :], AF.Square),
                           srcB.rd() + tdeps)
                tB.wtok(at)
                m1t = sch.t(mm(psu[0:1, :], csb["onescol128f"][:],
                               srcbuf[:, cc, :], cc == 0, cc == 7),
                            srcB.rd() + (rfree_l if cc == 0 else []))
                m2t = sch.t(mm(psq[0:1, :], csb["onescol128f"][:], tb[:],
                               cc == 0, cc == 7), [at])
                tB.rtok(m2t)
                mm_t += [m1t, m2t]
            rv = sch.v(ts(m_row[:], psu[0:1, :], 1.0 / D, None, ALU.mult),
                       mm_t)
            rv2 = sch.v(tt(var_row[:], m_row[:], m_row[:], ALU.mult), [rv])
            rv3 = sch.v(stt(var_row[:], psq[0:1, :], 1.0 / D, var_row[:],
                            ALU.mult, ALU.subtract), [rv2] + mm_t)
            ring.release(ri_l, [rv3])
            ra_ = sch.a(act(r_row[:], var_row[:], AF.Sqrt, bias=eps5[0:1, :]), [rv3])
            rv4 = sch.v(recip(r_row[:], r_row[:]), [ra_])
            rv4 = vgap()
            rv5 = sch.v(tt(mr_row[:], m_row[:], r_row[:], ALU.mult), [rv4])
            return rv4, rv5

        def ln_apply(srcbuf, srcB, r_tok, mr_tok, mode, dstbuf, dstB):
            ddeps = dstB.begin_write() if mode == "plain" else \
                dstB.begin_write()
            for cc in range(8):
                ri_a, rap_a, rfree_a = ring.acquire()
                pa_ = rap_a[:, 0:512]
                pb_ = rap_a[:, 512:1024]
                bt1 = sch.t(mm(pa_, csb["ones1x128f"][:], r_row[:],
                               True, True), [r_tok] + rfree_a, rg=(0, 1))
                bt2 = sch.t(mm(pb_, csb["ones1x128f"][:], mr_row[:],
                               True, True), [mr_tok], rg=(0, 1))
                tb = tmp0 if cc % 2 == 0 else tmp1
                tB = B["tmp0" if cc % 2 == 0 else "tmp1"]
                tdeps = tB.begin_write()
                v1 = sch.v(tt(tb[:], srcbuf[:, cc, :], pa_, ALU.mult),
                           [bt1] + srcB.rd() + tdeps)
                tB.wtok(v1)
                v2 = sch.v(tt(tb[:], tb[:], pb_, ALU.subtract), [v1, bt2])
                tB.wtok(v2)
                if mode == "res":
                    v3 = sch.v(tt(dstbuf[:, cc, :], dstbuf[:, cc, :], tb[:],
                                  ALU.add), [v2] + ddeps)
                else:
                    v3 = sch.v(cpy(dstbuf[:, cc, :], tb[:]), [v2] + ddeps)
                tB.rtok(v3)
                dstB.wtok(v3)
                ring.release(ri_a, [v3])

        rtk, mrtk = ln_stats(scr, B["scr"])
        ln_apply(scr, B["scr"], rtk, mrtk, "res", x0, B["x0"])
        if seg == DBGSEG:
            dbg_dump("d_x1", x0[:], B["x0"].rd(), B["x0"])

        _cut(80)
        rtk, mrtk = ln_stats(x0, B["x0"])
        ln_apply(x0, B["x0"], rtk, mrtk, "plain", scr, B["scr"])

        _cut(86)
        aqdeps = B["accq"].begin_write()
        at = sch.a(act(accq[:], scr[:, 0, :], AF.Abs),
                   B["scr"].rd() + aqdeps)
        B["accq"].wtok(at)
        for cc in range(1, 8):
            tb = tmp0 if cc % 2 == 0 else tmp1
            tB = B["tmp0" if cc % 2 == 0 else "tmp1"]
            tdeps = tB.begin_write()
            a2t = sch.a(act(tb[:], scr[:, cc, :], AF.Abs),
                        B["scr"].rd() + tdeps)
            tB.wtok(a2t)
            at = sch.v(tt(accq[:], accq[:], tb[:], ALU.max), [a2t, at])
            tB.rtok(at)
            B["accq"].wtok(at)
        if seg == DBGSEG:
            dbg_dump("d_x3", scr[:], B["scr"].rd(), B["scr"])
            dbg_dump("d_xqa", accq[:], B["accq"].rd(), B["accq"])
        ri27, rap27, rfree27 = ring.acquire()
        vt27 = rap27[:, 0:512].rearrange("p (a o) -> p a o", o=128)
        mts = []
        for i in range(4):
            mtx = sch.t(mm(vt27[:, i, :], accq[:, i * 128:(i + 1) * 128],
                           csb["I128f"][:], True, True),
                        [at] + consts_tok + (rfree27 if i == 0 else []))
            mts.append(mtx)
        if seg == DBGSEG and DBG and "d_xqp" in dbg_outs:
            t0deps = B["tmp0"].begin_write()
            cp_ = sch.v(cpy(tmp0[:], rap27[:, 0:512]), mts + t0deps)
            B["tmp0"].wtok(cp_)
            dbg_dump("d_xqp", tmp0[:], [cp_], B["tmp0"])
        rd2 = quant_colmax(ri27, rap27, mts)
        if seg == DBGSEG:
            dbg_dump("d_xqs", xst[:], B["xst"].rd(), B["xst"])
        ri28, rap28, rfree28 = ring.acquire()
        prow = rap28[:, 0:512]
        prow2 = rap28[:, 512:1024]
        mts = []
        for i in range(4):
            mtx = sch.t(mm(prow[0:1, i * 128:(i + 1) * 128], xst[:, i:i + 1],
                           csb["I128f"][:], True, True),
                        [rd2] + (rfree28 if i == 0 else []))
            mts.append(mtx)
        xrdeps = B["xsrow"].begin_write()
        xr1 = sch.v(cpy(xs_row[:], prow[0:1, :]), mts + xrdeps)
        B["xsrow"].wtok(xr1)
        ivdeps = B["ivrow"].begin_write()
        iv1 = sch.v(recip(iv_row[:], xs_row[:]), [xr1] + ivdeps)
        iv1 = vgap()
        iv2 = sch.v(ts(iv_row[:], iv_row[:], 127.0, None, ALU.mult), [iv1])
        B["ivrow"].wtok(iv2)
        if seg == DBGSEG and DBG and "d_xqr" in dbg_outs:
            dbg_dump_slot("d_xqr", 0, xs_row[:], [xr1])
            dbg_dump_slot("d_xqr", 1, iv_row[:], [iv2])
            B["xsrow"].rtok(_dbg_last[0]); B["ivrow"].rtok(_dbg_last[0])
        hsdeps = B["hsrow"].begin_write()
        hs1 = sch.v(ts(hs_row[:], xs_row[:], wsc[0:1, 0:1], 1.0 / 127.0,
                       ALU.mult, ALU.mult), [xr1, e4] + hsdeps)
        B["hsrow"].wtok(hs1)
        bt1 = sch.t(mm(prow2, csb["ones1x128f"][:], hs_row[:], True, True),
                    [hs1], rg=(0, 1))
        hbdeps = B["hscb"].begin_write()
        hb1 = sch.v(cpy(hscb[:], prow2), [bt1] + hbdeps)
        B["hscb"].wtok(hb1)
        bt2 = sch.t(mm(prow, csb["ones1x128f"][:], iv_row[:], True, True),
                    [iv2, xr1], rg=(0, 1))
        xqdeps = B["xn"].begin_write()
        for cc in range(8):
            tb = tmp0 if cc % 2 == 0 else tmp1
            tB = B["tmp0" if cc % 2 == 0 else "tmp1"]
            tdeps = tB.begin_write()
            v1 = sch.v(tt(tb[:], scr[:, cc, :], prow, ALU.mult),
                       [bt2] + B["scr"].rd() + tdeps)
            tB.wtok(v1)
            v2 = sch.v(ts(xn[:, cc, :], tb[:], MAGIC, -MAGIC, ALU.add,
                          ALU.add), [v1] + xqdeps)
            tB.rtok(v2)
            B["xn"].wtok(v2)
        ring.release(ri28, [v2, hb1])
        if seg == DBGSEG:
            dbg_dump("d_xq", xn[:], B["xn"].rd(), B["xn"])

        _cut(90)
        hhdeps = B["hh"].begin_write()
        for ef in range(8):
            view, lt, bb = wget("w1%d" % ef)
            for mr in range(4):
                mc = ef * 4 + mr
                ri29, rap29, rfree29 = ring.acquire()
                pp5 = rap29[:, 0:512]
                mt = None
                for kc in range(8):
                    mt = sch.t(mm(pp5, view[:, kc, mr * 128:(mr + 1) * 128],
                                  xn[:, kc, :], kc == 0, kc == 7),
                               [lt] + B["xn"].rd() +
                               (rfree29 if kc == 0 else []))
                bb.rtok(mt)
                tb = tmp0 if mc % 2 == 0 else tmp1
                tB = B["tmp0" if mc % 2 == 0 else "tmp1"]
                tdeps = tB.begin_write()
                v1 = sch.v(tt(tb[:], pp5, hscb[:], ALU.mult),
                           [mt, hb1] + tdeps)
                tB.wtok(v1)
                if seg == DBGSEG and mc < 6 and DBG and "d_w1t" in dbg_outs:
                    dbg_dump_slot("d_w1t", mc, tb[:], [v1])
                    tB.rtok(_dbg_last[0])
                if seg == DBGSEG and mc == 0 and DBG and "d_hsc" in dbg_outs:
                    dbg_dump("d_hsc", hscb[:], [hb1], B["hscb"])
                ring.release(ri29, [v1])
                a1_ = sch.a(act(hh[:, mc, :], tb[:], AF.Gelu,
                                bias=b1_s[:, mc:mc + 1]),
                            [v1, ldtok] + hhdeps)
                tB.rtok(a1_)
                B["hh"].wtok(a1_)
            wnext()

        _cut(100)
        ahdeps = B["acch"].begin_write()
        ht = sch.a(act(acch[:], hh[:, 0, :], AF.Abs), B["hh"].rd() + ahdeps)
        B["acch"].wtok(ht)
        for mc in range(1, 32):
            sq = sqpp[:, mc % 2, :]
            sqB = B["sq%d" % (mc % 2)]
            sdeps = sqB.begin_write()
            a2t = sch.a(act(sq, hh[:, mc, :], AF.Abs), B["hh"].rd() + sdeps)
            sqB.wtok(a2t)
            ht = sch.v(tt(acch[:], acch[:], sq, ALU.max), [a2t, ht])
            sqB.rtok(ht)
            B["acch"].wtok(ht)
        ri30, rap30, rfree30 = ring.acquire()
        vt30 = rap30[:, 0:512].rearrange("p (a o) -> p a o", o=128)
        mts = []
        for i in range(4):
            mtx = sch.t(mm(vt30[:, i, :], acch[:, i * 128:(i + 1) * 128],
                           csb["I128b"][:], True, True),
                        [ht] + consts_tok + (rfree30 if i == 0 else []))
            mts.append(mtx)
        rd2 = quant_colmax(ri30, rap30, mts)
        if seg == DBGSEG and DBG and "d_hh" in dbg_outs:
            dbg_dump("d_hh", hh[:], B["hh"].rd(), B["hh"])
            dbg_dump("d_hst", xst[:], B["xst"].rd(), B["xst"])
        ri31, rap31, rfree31 = ring.acquire()
        prow = rap31[:, 0:512]
        prow2 = rap31[:, 512:1024]
        mts = []
        for i in range(4):
            mtx = sch.t(mm(prow[0:1, i * 128:(i + 1) * 128], xst[:, i:i + 1],
                           csb["I128f"][:], True, True),
                        [rd2] + (rfree31 if i == 0 else []))
            mts.append(mtx)
        xrdeps = B["xsrow"].begin_write()
        xr1 = sch.v(cpy(xs_row[:], prow[0:1, :]), mts + xrdeps)
        B["xsrow"].wtok(xr1)
        ivdeps = B["ivrow"].begin_write()
        iv1 = sch.v(recip(iv_row[:], xs_row[:]), [xr1] + ivdeps)
        iv1 = vgap()
        iv2 = sch.v(ts(iv_row[:], iv_row[:], 127.0, None, ALU.mult), [iv1])
        B["ivrow"].wtok(iv2)
        hsdeps = B["hsrow"].begin_write()
        hs1 = sch.v(ts(hs_row[:], xs_row[:], wsc[0:1, 2:3], 1.0 / 127.0,
                       ALU.mult, ALU.mult), [xr1, e4] + hsdeps)
        B["hsrow"].wtok(hs1)
        bt1 = sch.t(mm(prow2, csb["ones1x128f"][:], hs_row[:], True, True),
                    [hs1], rg=(0, 1))
        hbdeps = B["hscb"].begin_write()
        hb1 = sch.v(cpy(hscb[:], prow2), [bt1] + hbdeps)
        B["hscb"].wtok(hb1)
        bt2 = sch.t(mm(prow, csb["ones1x128f"][:], iv_row[:], True, True),
                    [iv2, xr1], rg=(0, 1))
        if seg == DBGSEG and DBG and "d_hsr" in dbg_outs:
            dbg_dump_slot("d_hsr", 0, xs_row[:], [xr1])
            dbg_dump_slot("d_hsr", 1, iv_row[:], [iv2])
            B["xsrow"].rtok(_dbg_last[0]); B["ivrow"].rtok(_dbg_last[0])
        hqdeps = B["hh"].begin_write()
        last_hq = None
        for mc in range(32):
            tb = tmp0 if mc % 2 == 0 else tmp1
            tB = B["tmp0" if mc % 2 == 0 else "tmp1"]
            tdeps = tB.begin_write()
            v1 = sch.v(tt(tb[:], hh[:, mc, :], prow, ALU.mult),
                       [bt2] + tdeps + hqdeps)
            tB.wtok(v1)
            v2 = sch.v(ts(hh[:, mc, :], tb[:], MAGIC, -MAGIC, ALU.add,
                          ALU.add), [v1])
            tB.rtok(v2)
            B["hh"].wtok(v2)
            last_hq = v2
        ring.release(ri31, [last_hq, hb1])
        if seg == DBGSEG:
            dbg_dump("d_hq", hh[:], B["hh"].rd(), B["hh"])

        _cut(110)
        scrdeps = B["scr"].begin_write()
        for ef in range(8):
            view, lt, bb = wget("w2%d" % ef)
            ri32, rap32, rfree32 = ring.acquire()
            pp6 = rap32[:, 0:512]
            mt = None
            for kc in range(32):
                mt = sch.t(mm(pp6, view[:, kc, :], hh[:, kc, :],
                              kc == 0, kc == 31),
                           [lt] + B["hh"].rd() + (rfree32 if kc == 0 else []))
            bb.rtok(mt)
            tb = tmp0 if ef % 2 == 0 else tmp1
            tB = B["tmp0" if ef % 2 == 0 else "tmp1"]
            tdeps = tB.begin_write()
            v1 = sch.v(tt(tb[:], pp6, hscb[:], ALU.mult), [mt, hb1] + tdeps)
            tB.wtok(v1)
            ring.release(ri32, [v1])
            a1_ = sch.a(act(scr[:, ef, :], tb[:], AF.Identity,
                            bias=b2_s[:, ef:ef + 1]),
                        [v1, ldtok] + scrdeps)
            tB.rtok(a1_)
            B["scr"].wtok(a1_)
            wnext()

        _cut(120)
        rtk, mrtk = ln_stats(scr, B["scr"])
        ln_apply(scr, B["scr"], rtk, mrtk, "res", x0, B["x0"])

        ot = sch.dma("SO", dma(outr[:, :, tsl], x0[:]), B["x0"].rd())
        B["x0"].rtok(ot)

    except _StopBuild:
        pass
    fin = [("SO", sch.cnt["SO"])]
    if DBG:
        fin.append(("SD", sch.cnt["SD"]))
    sch.wait_only("S", fin)

    if os.environ.get("K_VERIFY", "0") == "1":
        sch.verify()
        sch.verify_banks()
    with nc.Block() as block:
        sch.emit(block)
    return nc


def nc_concat_rows(*args):
    return None



_NC_CACHE = {}


def _get_nc():
    if "nc" not in _NC_CACHE:
        _NC_CACHE["nc"] = build_nc()
    return _NC_CACHE["nc"]


def kernel(**inputs):
    x = np.asarray(inputs["L0_attn_hidden_states"], dtype=np.float32)
    q_w = np.asarray(inputs["L0_attn_q_w"], dtype=np.float32)
    k_w = np.asarray(inputs["L0_attn_k_w"], dtype=np.float32)
    v_w = np.asarray(inputs["L0_attn_v_w"], dtype=np.float32)
    a_w = np.asarray(inputs["L0_attn_a_w"], dtype=np.float32)
    b_w = np.asarray(inputs["L0_attn_b_w"], dtype=np.float32)
    g_w = np.asarray(inputs["L0_attn_g_w"], dtype=np.float32)
    dt_bias = np.asarray(inputs["L0_attn_dt_bias"], dtype=np.float32)
    A_log = np.asarray(inputs["L0_attn_A_log"], dtype=np.float32)
    cq = np.asarray(inputs["L0_attn_conv_q_w"], dtype=np.float32)
    ck = np.asarray(inputs["L0_attn_conv_k_w"], dtype=np.float32)
    cv = np.asarray(inputs["L0_attn_conv_v_w"], dtype=np.float32)
    norm_w = np.asarray(inputs["L0_attn_norm_w"], dtype=np.float32)
    o_norm_w = np.asarray(inputs["L0_attn_o_norm_w"], dtype=np.float32)
    o_proj_w = np.asarray(inputs["L0_attn_o_proj_w"], dtype=np.float32)
    out_proj_w = np.asarray(inputs["L0_attn_out_proj_w"], dtype=np.float32)
    W1 = np.asarray(inputs["L0_ffn_W1"], dtype=np.float32)
    b1 = np.asarray(inputs["L0_ffn_b1"], dtype=np.float32)
    W2 = np.asarray(inputs["L0_ffn_W2"], dtype=np.float32)
    b2 = np.asarray(inputs["L0_ffn_b2"], dtype=np.float32)

    ab = np.zeros((64, D), dtype=np.float32)
    ab[0:16] = a_w
    ab[32:48] = b_w
    cw = np.concatenate([cq, ck, cv], axis=0)
    cw3 = np.ascontiguousarray(cw.reshape(32, 128, 4).transpose(1, 0, 2))

    shared = {
        "qwT": np.ascontiguousarray(q_w.T),
        "kwT": np.ascontiguousarray(k_w.T),
        "vwT": np.ascontiguousarray(v_w.T),
        "gwT": np.ascontiguousarray(g_w.T),
        "abT": np.ascontiguousarray(ab.T),
        "opT": np.ascontiguousarray(o_proj_w.T),
        "ouT": np.ascontiguousarray(out_proj_w.T),
        "w1T": np.ascontiguousarray(W1.T),
        "w2T": np.ascontiguousarray(
            W2.T.reshape(32, 128, 8, 128).transpose(1, 2, 0, 3)),
        "cw": cw3,
        "b1c": np.ascontiguousarray(b1.reshape(32, 128).T),
        "b2c": np.ascontiguousarray(b2.reshape(8, 128).T),
        "nwrow": np.ascontiguousarray(norm_w.reshape(1, D)),
        "onw": np.ascontiguousarray(o_norm_w.reshape(128, 1)),
        "dtb": np.ascontiguousarray(dt_bias.reshape(16, 1)),
        "alog": np.ascontiguousarray(A_log.reshape(16, 1)),
    }
    in_maps = []
    for b in range(8):
        m = dict(shared)
        m["xT"] = np.ascontiguousarray(x[b].T)
        in_maps.append(m)

    nc = _get_nc()
    res = run_bass_kernel_spmd(nc, in_maps, core_ids=list(range(8)))
    out = np.stack([np.asarray(r["outT"]).T for r in res.results], axis=0)
    return out.astype(np.float32)


if __name__ == "__main__":
    import reference as _ref
    inputs = _ref.setup_inputs()
    out = kernel(**{k: np.asarray(v) for k, v in inputs.items()})
    ref = np.asarray(_ref.reference(**inputs))
    err = np.linalg.norm(out - ref) / np.linalg.norm(ref)
    print("rel err:", err)
